# version 01 (final)
# baseline (speedup 1.0000x reference)
import math
import contextlib
import numpy as np
import ml_dtypes
import concourse.bass as bass
import concourse.mybir as mybir
from concourse.bass_utils import run_bass_kernel_spmd

F32 = mybir.dt.float32
BF16 = mybir.dt.bfloat16
U8 = mybir.dt.uint8
AF = mybir.ActivationFunctionType
ALU = mybir.AluOpType
AX = mybir.AxisListType

NCORES = 8
T = 2048
NSEQ = 2
NTOK = T * NSEQ
D = 1024
NCH = D // 128
PAN = 512
BF = ml_dtypes.bfloat16


class P:
    def __init__(self, nc):
        self.nc = nc
        self.es = contextlib.ExitStack()
        self.sem = self.es.enter_context(nc.semaphore("SER"))
        self.total = 0
        self.q = {"sync": [], "scalar": [], "vector": [], "tensor": [], "gpsimd": []}
        self._n = 0
        self.cur_es = self.es

    @contextlib.contextmanager
    def scope(self):
        old = self.cur_es
        with contextlib.ExitStack() as es:
            self.cur_es = es
            try:
                yield es
            finally:
                self.cur_es = old

    def sb(self, name, shape, dtype=F32, es=None):
        return (es or self.cur_es).enter_context(
            self.nc.sbuf_tensor(f"{name}_{self._bump()}", shape, dtype))

    def ps(self, name, shape, dtype=F32, es=None):
        return (es or self.cur_es).enter_context(
            self.nc.psum_tensor(f"{name}_{self._bump()}", shape, dtype))

    def _bump(self):
        self._n += 1
        return self._n

    def emit(self, eng, fn, inc=1):
        wait = self.total
        self.total += inc
        self.q[eng].append((wait, fn, inc))

    def dma(self, out, in_, eng="sync", **kw):
        self.emit(eng, lambda e, out=out, in_=in_, kw=kw: e.dma_start(out=out, in_=in_, **kw), inc=16)

    def v(self, name, *a, **kw):
        self.emit("vector", lambda e, a=a, kw=kw: getattr(e, name)(*a, **kw))

    def s(self, name, *a, **kw):
        self.emit("scalar", lambda e, a=a, kw=kw: getattr(e, name)(*a, **kw))

    def t(self, name, *a, **kw):
        self.emit("tensor", lambda e, a=a, kw=kw: getattr(e, name)(*a, **kw))

    def g(self, name, *a, **kw):
        self.emit("gpsimd", lambda e, a=a, kw=kw: getattr(e, name)(*a, **kw))

    def finalize(self):
        nc = self.nc
        with nc.Block() as blk:
            def mk(name):
                def body(eng):
                    for (wait, fn, inc) in self.q[name]:
                        if wait > 0:
                            eng.wait_ge(self.sem, wait)
                        fn(eng).then_inc(self.sem, inc)
                    if name == "sync":
                        eng.wait_ge(self.sem, self.total)
                return body
            blk.sync(mk("sync"))
            blk.scalar(mk("scalar"))
            blk.vector(mk("vector"))
            blk.tensor(mk("tensor"))
            blk.gpsimd(mk("gpsimd"))
        self.es.close()


def new_prog():
    nc = bass.Bass("TRN2", target_bir_lowering=False)
    xin = nc.dram_tensor("xin", [D, NTOK], F32, kind="ExternalInput").ap()
    xout = nc.dram_tensor("xout", [D, NTOK], F32, kind="ExternalOutput").ap()
    fout = nc.dram_tensor("fout", [D, NTOK], F32, kind="Internal").ap()
    p = P(nc)
    idn_d = nc.dram_tensor("aux_idn", [128, 128], BF16, kind="ExternalInput").ap()
    ones_d = nc.dram_tensor("aux_ones", [128, 128], F32, kind="ExternalInput").ap()
    aux = {}
    aux["identity"] = p.sb("aux_idn", [128, 128], BF16)
    aux["ones128"] = p.sb("aux_ones", [128, 128], F32)
    aux["ones1_bf"] = p.sb("aux_ones1", [1, 128], BF16)
    aux["onescol_bf"] = p.sb("aux_onesc", [128, 1], BF16)
    p.dma(aux["identity"][:], idn_d)
    p.dma(aux["ones128"][:], ones_d)
    p.v("memset", aux["ones1_bf"][:], 1.0)
    p.v("memset", aux["onescol_bf"][:], 1.0)
    return nc, p, xin, xout, fout, aux


def aux_arrays():
    return {"aux_idn": np.eye(128).astype(BF),
            "aux_ones": np.ones((128, 128), np.float32)}


def din(nc, name, shape, dtype=F32):
    return nc.dram_tensor(name, shape, dtype, kind="ExternalInput").ap()


def load_x_bf(p, xin, ncols=NTOK, col0=0, es=None, stage=None):
    if stage is None:
        stage = p.sb("lx_stage", [128, ncols], F32, es=es)
    xbf = [p.sb(f"xbf{c}", [128, ncols], BF16, es=es) for c in range(NCH)]
    for c in range(NCH):
        p.dma(stage[:, 0:ncols], xin[c * 128:(c + 1) * 128, col0:col0 + ncols])
        p.v("tensor_copy", xbf[c][:], stage[:, 0:ncols])
    return xbf


class WStrip:

    def __init__(self, p, kc_n, mw, name="w", es=None, with_stage=True):
        self.p = p
        self.kc_n = kc_n
        self.mw = mw
        if with_stage:
            self.stage = p.sb(f"{name}_stg", [128, kc_n * mw], F32, es=es)
        self.buf = p.sb(f"{name}_bf", [128, kc_n * mw], BF16, es=es)

    def load(self, wdram, m0, mw=None, k0=0, kc_n=None):
        kc_n = kc_n if kc_n is not None else self.kc_n
        mw = mw if mw is not None else self.mw
        p = self.p
        src = wdram[k0:k0 + kc_n * 128, m0:m0 + mw].rearrange("(kc q) m -> q kc m", q=128)
        dst = self.stage[:, 0:kc_n * mw].rearrange("q (kc m) -> q kc m", kc=kc_n, m=mw)
        p.dma(dst, src)
        p.v("tensor_copy", self.buf[:, 0:kc_n * mw], self.stage[:, 0:kc_n * mw])
        self.cur_mw = mw

    def tile(self, kc, j0=0, jw=None):
        mw = getattr(self, "cur_mw", self.mw)
        jw = jw if jw is not None else min(mw, 128)
        return self.buf[:, kc * mw + j0: kc * mw + j0 + jw]


def mm_chunks(p, psum, wstrip, rhs_chunks, n0, nw, mtile=(0, 128), kc_list=None,
              start=True, stop=True):
    kl = list(kc_list if kc_list is not None else range(wstrip.kc_n))
    m0, mw2 = mtile
    for i, kc in enumerate(kl):
        p.t("matmul", psum, wstrip.tile(kc, m0, mw2), rhs_chunks[kc][:, n0:n0 + nw],
            start=(start and i == 0), stop=(stop and i == len(kl) - 1))


def col_tiles(p, dram_vec, n, name, es=None):
    if n <= 128:
        t = p.sb(name, [n, 1], F32, es=es)
        p.dma(t[:], dram_vec.rearrange("(q one) -> q one", one=1))
        return [t]
    out = []
    for c in range(n // 128):
        t = p.sb(f"{name}{c}", [128, 1], F32, es=es)
        p.dma(t[:], dram_vec[c * 128:(c + 1) * 128].rearrange("(q one) -> q one", one=1))
        out.append(t)
    return out


def ln_epilogue(p, xin, fsrc, xout, aux):
    ones128 = aux["ones128"]
    with contextlib.ExitStack() as es:
        pa = p.ps("lnA", [128, PAN], es=es)
        pb = p.ps("lnB", [128, PAN], es=es)
        s_all = p.sb("ln_s", [128, NCH * PAN], F32, es=es)
        t1 = p.sb("ln_t1", [128, PAN], F32, es=es)
        t2 = p.sb("ln_t2", [128, PAN], F32, es=es)
        t3 = p.sb("ln_t3", [128, PAN], F32, es=es)
        teps = p.sb("ln_eps", [128, 1], F32, es=es)
        p.v("memset", teps[:], 1e-5)
        for pan in range(NTOK // PAN):
            c0, c1 = pan * PAN, (pan + 1) * PAN
            for c in range(NCH):
                s_c = s_all[:, c * PAN:(c + 1) * PAN]
                p.dma(t1[:], xin[c * 128:(c + 1) * 128, c0:c1])
                p.dma(t2[:], fsrc[c * 128:(c + 1) * 128, c0:c1])
                p.v("tensor_add", s_c, t1[:], t2[:])
                p.t("matmul", pa[:], ones128[:], s_c, start=(c == 0), stop=(c == NCH - 1))
                p.s("square", t3[:], s_c)
                p.t("matmul", pb[:], ones128[:], t3[:], start=(c == 0), stop=(c == NCH - 1))
            p.v("tensor_scalar_mul", t1[:], pa[:], 1.0 / D)
            p.v("tensor_scalar_mul", t2[:], pb[:], 1.0 / D)
            p.v("tensor_mul", t3[:], t1[:], t1[:])
            p.v("tensor_sub", t2[:], t2[:], t3[:])
            p.s("activation", t2[:], t2[:], AF.Ln, bias=teps[:])
            p.s("activation", t2[:], t2[:], AF.Exp, scale=-0.5)
            for c in range(NCH):
                s_c = s_all[:, c * PAN:(c + 1) * PAN]
                p.v("tensor_sub", s_c, s_c, t1[:])
                p.v("tensor_mul", s_c, s_c, t2[:])
                p.dma(xout[c * 128:(c + 1) * 128, c0:c1], s_c)



def build_L0_attn():
    nc, p, xin, xout, fout, aux = new_prog()
    gw1 = din(nc, "gw1", [D, 256]); gb1 = din(nc, "gb1", [256])
    gw2 = din(nc, "gw2", [256, 32]); gb2 = din(nc, "gb2", [32])
    uw = din(nc, "uw", [D, 32]); vw = din(nc, "vw", [D, 32])
    convw = din(nc, "convw", [3, D])
    outw = din(nc, "outw", [32, D]); outb = din(nc, "outb", [D])

    body = p.scope(); body.__enter__()
    xbf = load_x_bf(p, xin)
    w1 = WStrip(p, NCH, 256, "w1"); w1.load(gw1, 0)
    w2 = WStrip(p, 2, 32, "w2"); w2.load(gw2, 0)
    wu = WStrip(p, NCH, 32, "wu"); wu.load(uw, 0)
    wv = WStrip(p, NCH, 32, "wv"); wv.load(vw, 0)
    wo_st = p.sb("wo_st", [32, D], F32)
    wo = p.sb("wo", [32, D], BF16)
    p.dma(wo_st[:], outw)
    p.v("tensor_copy", wo[:], wo_st[:])
    gb1_t = col_tiles(p, gb1, 256, "gb1t")
    gb2_t = col_tiles(p, gb2, 32, "gb2t")
    outb_t = col_tiles(p, outb, D, "outbt")
    cw = [col_tiles(p, convw[j, :], D, f"cw{j}") for j in range(3)]

    pA = p.ps("pA", [128, PAN])
    pC = p.ps("pC", [32, PAN])
    hbf = [p.sb(f"h{m}", [128, NTOK], BF16) for m in range(2)]
    gate = p.sb("gate", [32, NTOK], BF16)
    pack1 = p.sb("pack1", [128, NTOK], F32)
    pack2 = p.sb("pack2", [128, NTOK], F32)
    uu = pack1[0:32, :]
    vv = pack1[64:96, :]
    ss = pack2[0:32, :]
    zer32 = pack2[64:96, :]
    gus = p.sb("gus", [32, NTOK], BF16)
    t512 = p.sb("t512", [128, PAN], F32)
    p.v("memset", zer32[:], 0.0)

    NP = NTOK // PAN
    for m in range(2):
        for n in range(NP):
            mm_chunks(p, pA[:], w1, xbf, n * PAN, PAN, mtile=(m * 128, 128))
            p.s("activation", hbf[m][:, n * PAN:(n + 1) * PAN], pA[:], AF.Relu,
                bias=gb1_t[m][:])
    for n in range(NP):
        mm_chunks(p, pC[:], w2, hbf, n * PAN, PAN, mtile=(0, 32))
        p.s("activation", gate[0:32, n * PAN:(n + 1) * PAN], pC[:], AF.Sigmoid,
            bias=gb2_t[0][:])
    for n in range(NP):
        mm_chunks(p, pC[:], wu, xbf, n * PAN, PAN, mtile=(0, 32))
        p.v("tensor_copy", uu[:, n * PAN:(n + 1) * PAN], pC[:])
        mm_chunks(p, pC[:], wv, xbf, n * PAN, PAN, mtile=(0, 32))
        p.v("tensor_copy", vv[:, n * PAN:(n + 1) * PAN], pC[:])
    for sq in range(NSEQ):
        sl = slice(sq * T, (sq + 1) * T)
        p.v("tensor_tensor_scan", ss[:, sl], vv[:, sl], zer32[:, sl], 0.0,
            ALU.add, ALU.add)
    p.v("tensor_mul", uu[:, :], uu[:, :], ss[:, :])
    p.v("tensor_mul", gus[0:32, :], uu[:, :], gate[0:32, :])
    for c in range(NCH):
        for n in range(NP):
            p.t("matmul", pA[:], wo[0:32, c * 128:(c + 1) * 128],
                gus[0:32, n * PAN:(n + 1) * PAN], start=True, stop=True)
            p.v("tensor_scalar_add", t512[:], pA[:], outb_t[c][:])
            p.dma(fout[c * 128:(c + 1) * 128, n * PAN:(n + 1) * PAN], t512[:])
    xc = p.sb("xc", [128, T], F32)
    fc = p.sb("fc", [128, T], F32)
    for c in range(NCH):
        for sq in range(NSEQ):
            sl = slice(sq * T, (sq + 1) * T)
            p.dma(xc[:], xin[c * 128:(c + 1) * 128, sl])
            p.dma(fc[:], fout[c * 128:(c + 1) * 128, sl])
            p.v("scalar_tensor_tensor", fc[:], xc[:], cw[1][c][:], fc[:], ALU.mult, ALU.add)
            p.v("scalar_tensor_tensor", fc[:, 1:T], xc[:, 0:T - 1], cw[0][c][:],
                fc[:, 1:T], ALU.mult, ALU.add)
            p.v("scalar_tensor_tensor", fc[:, 0:T - 1], xc[:, 1:T], cw[2][c][:],
                fc[:, 0:T - 1], ALU.mult, ALU.add)
            p.dma(fout[c * 128:(c + 1) * 128, sl], fc[:])
    body.__exit__(None, None, None)
    ln_epilogue(p, xin, fout, xout, aux)
    p.finalize()
    return nc



def build_L0_ffn(s0, s1, s2):
    nc, p, xin, xout, fout, aux = new_prog()
    k0 = din(nc, "k0", [2 * D, 4096], U8)
    k1 = din(nc, "k1", [4096, 4096], U8)
    k2 = din(nc, "k2", [4096, D], U8)
    TQ = 1024
    NQ = NTOK // TQ
    NPQ = TQ // PAN

    stg_u8a = p.sb("stgu8a", [128, 8 * 128], U8)
    stg_u8b = p.sb("stgu8b", [128, 8 * 128], U8)
    w0a = p.sb("w0a", [128, 8 * 128], F32)
    w0b = p.sb("w0b", [128, 8 * 128], F32)
    stg_u8c = p.sb("stgu8c", [128, 32 * 128], U8)
    wt = p.sb("wt", [128, 32 * 128], BF16)
    pA = p.ps("pA", [128, PAN])
    t512 = p.sb("t512l0", [128, PAN], F32)

    for tq in range(NQ):
        with contextlib.ExitStack() as es:
            xbf = load_x_bf(p, xin, TQ, tq * TQ, es=es)
            h1 = [p.sb(f"h1_{m}", [128, TQ], BF16, es=es) for m in range(32)]
            for m in range(32):
                src_a = k0[0:D, m * 128:(m + 1) * 128].rearrange("(kc q) m -> q kc m", q=128)
                src_b = k0[D:2 * D, m * 128:(m + 1) * 128].rearrange("(kc q) m -> q kc m", q=128)
                p.dma(stg_u8a[:].rearrange("q (kc m) -> q kc m", kc=8, m=128), src_a)
                p.dma(stg_u8b[:].rearrange("q (kc m) -> q kc m", kc=8, m=128), src_b)
                p.v("tensor_copy", w0a[:], stg_u8a[:])
                p.v("tensor_copy", w0b[:], stg_u8b[:])
                p.v("tensor_sub", w0a[:], w0a[:], w0b[:])
                p.v("tensor_scalar_mul", wt[:, 0:8 * 128], w0a[:], 2.0 * s0)
                for n in range(NPQ):
                    for kc in range(8):
                        p.t("matmul", pA[:], wt[:, kc * 128:(kc + 1) * 128],
                            xbf[kc][:, n * PAN:(n + 1) * PAN],
                            start=(kc == 0), stop=(kc == 7))
                    p.s("activation", h1[m][:, n * PAN:(n + 1) * PAN], pA[:], AF.Relu)
            h2 = [p.sb(f"h2_{m}", [128, TQ], BF16, es=es) for m in range(32)]
            for m in range(32):
                src = k1[:, m * 128:(m + 1) * 128].rearrange("(kc q) m -> q kc m", q=128)
                p.dma(stg_u8c[:].rearrange("q (kc m) -> q kc m", kc=32, m=128), src)
                p.v("tensor_scalar", wt[:], stg_u8c[:], 4.0 * s1, -2.0 * s1,
                    ALU.mult, ALU.add)
                for n in range(NPQ):
                    for kc in range(32):
                        p.t("matmul", pA[:], wt[:, kc * 128:(kc + 1) * 128],
                            h1[kc][:, n * PAN:(n + 1) * PAN],
                            start=(kc == 0), stop=(kc == 31))
                    p.s("activation", h2[m][:, n * PAN:(n + 1) * PAN], pA[:], AF.Relu)
            for m in range(NCH):
                src = k2[:, m * 128:(m + 1) * 128].rearrange("(kc q) m -> q kc m", q=128)
                p.dma(stg_u8c[:].rearrange("q (kc m) -> q kc m", kc=32, m=128), src)
                p.v("tensor_scalar", wt[:], stg_u8c[:], 4.0 * s2, -2.0 * s2,
                    ALU.mult, ALU.add)
                for n in range(NPQ):
                    for kc in range(32):
                        p.t("matmul", pA[:], wt[:, kc * 128:(kc + 1) * 128],
                            h2[kc][:, n * PAN:(n + 1) * PAN],
                            start=(kc == 0), stop=(kc == 31))
                    p.v("tensor_copy", t512[:], pA[:])
                    p.dma(fout[m * 128:(m + 1) * 128,
                               tq * TQ + n * PAN: tq * TQ + (n + 1) * PAN], t512[:])
    ln_epilogue(p, xin, fout, xout, aux)
    p.finalize()
    return nc



def l1a_aux():
    tri = np.triu(np.ones((128, 128), np.float32)).astype(BF)
    return {"aux_triu": tri}


def build_L1_attn():
    nc, p, xin, xout, fout, aux = new_prog()
    wqkv = din(nc, "wqkv", [D, 3 * D])
    battn = din(nc, "battn", [3 * D])
    wproj = din(nc, "wproj", [D, D])
    bproj = din(nc, "bproj", [D])
    trid = nc.dram_tensor("aux_triu", [128, 128], BF16, kind="ExternalInput").ap()
    NH, DH, CH, WIN = 16, 64, 128, 512
    EPS = 1e-6

    tri_bf = p.sb("tri", [128, 128], BF16)
    p.dma(tri_bf[:], trid)
    battn_t = col_tiles(p, battn, 3 * D, "battn")
    brow_st = p.sb("brow_st", [1, 3 * D], F32)
    brow = p.sb("brow", [1, 3 * D], BF16)
    p.dma(brow_st[:], battn.rearrange("(one n) -> one n", one=1))
    p.v("tensor_copy", brow[:], brow_st[:])
    bproj_t = col_tiles(p, bproj, D, "bproj")

    for sq in range(NSEQ):
        with p.scope():
            base = sq * T
            xbf = load_x_bf(p, xin, T, base)
            yfm = [p.sb(f"y{c}", [128, T], BF16) for c in range(NCH)]
            st_kv = [p.sb(f"stkv{h}", [64, 65], F32) for h in range(NH)]
            st_bf = [p.sb(f"stbf{h}", [64, 65], BF16) for h in range(NH)]
            for h in range(NH):
                p.v("memset", st_kv[h][:], 0.0)
                p.v("memset", st_bf[h][:], 0.0)
            wq = WStrip(p, NCH, 128, "wqs")
            wv5 = WStrip(p, NCH, 512, "wvs")
            pP = p.ps("pP", [128, PAN])
            qf = [p.sb(f"qf{c}", [128, WIN], BF16) for c in range(NCH)]
            kf = [p.sb(f"kf{c}", [128, WIN], BF16) for c in range(NCH)]
            vtm = [p.sb(f"vtm{i}", [128, D], BF16) for i in range(WIN // CH)]
            tq1 = p.sb("tq1", [128, WIN], F32)
            tq2 = p.sb("tq2", [128, WIN], F32)
            pS = p.ps("pS", [128, 128])
            pO = p.ps("pO", [64, 128])
            pD = p.ps("pD", [1, 128])
            pDb = p.ps("pDb", [64, 128])
            pT = p.ps("pT", [128, 64], BF16)
            pD2 = p.ps("pD2", [64, 65])
            pt_bf = p.sb("ptbf", [128, 128], BF16)
            ktm_bf = p.sb("ktmbf", [128, 64], BF16)
            q0 = p.sb("q0", [64, 128], BF16)
            k0 = p.sb("k0", [64, 128], BF16)
            o_sb = p.sb("osb", [64, 128], F32)
            den = p.sb("den", [1, 128], F32)
            denb = p.sb("denb", [1, 128], BF16)

            for wi in range(T // WIN):
                w0 = wi * WIN
                for dst, bofs in ((qf, 0), (kf, D)):
                    for m in range(NCH):
                        wq.load(wqkv, bofs + m * 128)
                        mm_chunks(p, pP[:], wq, xbf, w0, WIN, mtile=(0, 128))
                        p.v("tensor_scalar_add", tq1[:], pP[:],
                            battn_t[(bofs + m * 128) // 128][:])
                        p.v("tensor_scalar_min", tq2[:], tq1[:], 0.0)
                        p.s("activation", tq2[:], tq2[:], AF.Exp)
                        p.v("tensor_scalar_max", tq1[:], tq1[:], 0.0)
                        p.v("tensor_add", tq1[:], tq1[:], tq2[:])
                        p.v("tensor_copy", dst[m][:], tq1[:])
                for i in range(WIN // CH):
                    for n2 in range(2):
                        wv5.load(wqkv, 2 * D + n2 * 512)
                        p.t("matmul", pP[:], aux["ones1_bf"][:],
                            brow[0:1, 2 * D + n2 * 512: 2 * D + (n2 + 1) * 512],
                            start=True, stop=False)
                        for kc in range(NCH):
                            p.t("matmul", pP[:], xbf[kc][:, w0 + i * CH: w0 + (i + 1) * CH],
                                wv5.tile(kc, 0, 512), start=False, stop=(kc == NCH - 1))
                        p.v("tensor_copy", vtm[i][:, n2 * 512:(n2 + 1) * 512], pP[:])
                for i in range(WIN // CH):
                    gi = w0 + i * CH
                    for h in range(NH):
                        p.v("tensor_copy", q0[:],
                            qf[h // 2][(h % 2) * 64:(h % 2) * 64 + 64, i * CH:(i + 1) * CH])
                        p.v("tensor_copy", k0[:],
                            kf[h // 2][(h % 2) * 64:(h % 2) * 64 + 64, i * CH:(i + 1) * CH])
                        qh = q0[:]
                        kh = k0[:]
                        vh = vtm[i][:, h * 64:(h + 1) * 64]
                        p.t("matmul", pS[:], kh, qh, start=True, stop=True)
                        p.v("tensor_mul", pt_bf[:], pS[:], tri_bf[:])
                        p.t("matmul", pO[:], vh, pt_bf[:], start=True, stop=False)
                        p.t("matmul", pO[:], st_bf[h][0:64, 0:64], qh, start=False, stop=True)
                        p.t("matmul", pD[:], aux["onescol_bf"][:], pt_bf[:],
                            start=True, stop=False)
                        p.t("matmul", pD[:], st_bf[h][0:64, 64:65], qh,
                            start=False, stop=True)
                        p.v("tensor_scalar_add", den[:], pD[:], EPS)
                        p.v("reciprocal", den[:], den[:])
                        p.v("tensor_copy", denb[:], den[:])
                        p.t("matmul", pDb[:], aux["ones1_bf"][0:1, 0:64], denb[:],
                            start=True, stop=True)
                        p.v("tensor_copy", o_sb[:], pO[:])
                        p.v("tensor_mul", yfm[h // 2][(h % 2) * 64:(h % 2) * 64 + 64,
                                                      gi:gi + CH], o_sb[:], pDb[:])
                        p.t("transpose", pT[:], kh, aux["identity"][0:64, 0:64])
                        p.v("tensor_copy", ktm_bf[:], pT[:])
                        p.t("matmul", pD2[0:64, 0:64], ktm_bf[:], vh, start=True, stop=True)
                        p.t("matmul", pD2[0:64, 64:65], ktm_bf[:], aux["onescol_bf"][:],
                            start=True, stop=True)
                        p.v("tensor_add", st_kv[h][:], st_kv[h][:], pD2[:])
                        p.v("tensor_copy", st_bf[h][:], st_kv[h][:])
            wpj = WStrip(p, NCH, 128, "wpj")
            t512 = p.sb("t512", [128, PAN], F32)
            for m in range(NCH):
                wpj.load(wproj, m * 128)
                for n in range(T // PAN):
                    mm_chunks(p, pP[:], wpj, yfm, n * PAN, PAN, mtile=(0, 128))
                    p.v("tensor_scalar_add", t512[:], pP[:], bproj_t[m][:])
                    p.dma(fout[m * 128:(m + 1) * 128, base + n * PAN: base + (n + 1) * PAN],
                          t512[:])
    ln_epilogue(p, xin, fout, xout, aux)
    p.finalize()
    return nc



def build_L1_ffn():
    nc, p, xin, xout, fout, aux = new_prog()
    wrT = din(nc, "wrT", [D, D])
    wkT = din(nc, "wkT", [D, D])
    wvT = din(nc, "wvT", [D, D])
    mur = din(nc, "mur", [D])
    muk = din(nc, "muk", [D])

    mur_t = col_tiles(p, mur, D, "murt")
    muk_t = col_tiles(p, muk, D, "mukt")
    omr_t = [p.sb(f"omr{c}", [128, 1], F32) for c in range(NCH)]
    omk_t = [p.sb(f"omk{c}", [128, 1], F32) for c in range(NCH)]
    for c in range(NCH):
        p.v("tensor_scalar", omr_t[c][:], mur_t[c][:], -1.0, 1.0, ALU.mult, ALU.add)
        p.v("tensor_scalar", omk_t[c][:], muk_t[c][:], -1.0, 1.0, ALU.mult, ALU.add)

    for sq in range(NSEQ):
        with p.scope():
            base = sq * T
            xr = [p.sb(f"xr{c}", [128, T], BF16) for c in range(NCH)]
            xk = [p.sb(f"xk{c}", [128, T], BF16) for c in range(NCH)]
            with p.scope():
                xf = p.sb("xf", [128, T], F32)
                tmp = p.sb("tmp", [128, T], F32)
                for c in range(NCH):
                    p.dma(xf[:], xin[c * 128:(c + 1) * 128, base:base + T])
                    for dst, mu, om in ((xr, mur_t, omr_t), (xk, muk_t, omk_t)):
                        p.v("tensor_scalar_mul", tmp[:], xf[:], mu[c][:])
                        p.v("tensor_copy", dst[c][:, 0:1], tmp[:, 0:1])
                        p.v("scalar_tensor_tensor", dst[c][:, 1:T], xf[:, 0:T - 1],
                            om[c][:], tmp[:, 1:T], ALU.mult, ALU.add)
            rbf = [p.sb(f"r{m}", [128, T], BF16) for m in range(NCH)]
            kact = [p.sb(f"ka{m}", [128, T], BF16) for m in range(NCH)]
            pA = p.ps("pA", [128, PAN])
            trl = p.sb("trl", [128, PAN], F32)
            ws = WStrip(p, NCH, 128, "ws")
            NPS = T // PAN
            for m in range(NCH):
                ws.load(wrT, m * 128)
                for n in range(NPS):
                    mm_chunks(p, pA[:], ws, xr, n * PAN, PAN, mtile=(0, 128))
                    p.s("activation", rbf[m][:, n * PAN:(n + 1) * PAN], pA[:], AF.Sigmoid)
            for m in range(NCH):
                ws.load(wkT, m * 128)
                for n in range(NPS):
                    mm_chunks(p, pA[:], ws, xk, n * PAN, PAN, mtile=(0, 128))
                    p.s("activation", trl[:], pA[:], AF.Relu)
                    p.v("tensor_mul", kact[m][:, n * PAN:(n + 1) * PAN], trl[:], trl[:])
            for m in range(NCH):
                ws.load(wvT, m * 128)
                for n in range(NPS):
                    mm_chunks(p, pA[:], ws, kact, n * PAN, PAN, mtile=(0, 128))
                    p.v("tensor_mul", trl[:], pA[:], rbf[m][:, n * PAN:(n + 1) * PAN])
                    p.dma(fout[m * 128:(m + 1) * 128, base + n * PAN: base + (n + 1) * PAN],
                          trl[:])
    ln_epilogue(p, xin, fout, xout, aux)
    p.finalize()
    return nc



def l2a_aux():
    hd = 64
    inv = 1.0 / (10000.0 ** (np.arange(0, hd, 2, dtype=np.float64) / hd))
    t = np.arange(T, dtype=np.float64)
    fr = np.outer(t, inv)
    cos = np.cos(fr).T.astype(np.float32)
    sin = np.sin(fr).T.astype(np.float32)
    COS = np.zeros((128, T), np.float32)
    SIN = np.zeros((128, T), np.float32)
    for g in range(4):
        COS[g * 32:(g + 1) * 32] = cos
        sgn = -1.0 if (g % 2 == 0) else 1.0
        SIN[g * 32:(g + 1) * 32] = sgn * sin
    perm = np.zeros((128, 128), np.float32)
    for m in range(128):
        half = (m % 64) // 32 * 0
    for m in range(128):
        grp = m // 64
        off = m % 64
        src = grp * 64 + ((off + 32) % 64)
        perm[src, m] = 1.0
    esel = np.zeros((128, 2), np.float32)
    esel[0:64, 0] = 1.0
    esel[64:128, 1] = 1.0
    bsel = np.zeros((2, 128), np.float32)
    bsel[0, 0:64] = 1.0
    bsel[1, 64:128] = 1.0
    masks = np.zeros((4, 128, PAN), np.float32)
    for v in range(4):
        for qi in range(128):
            masks[v, qi, v * 128 + qi + 1:] = -1e30
    out = {"aux_cos": COS.astype(BF), "aux_sin": SIN.astype(BF),
           "aux_perm": perm.astype(BF), "aux_esel": esel,
           "aux_bsel": bsel.astype(BF)}
    for v in range(4):
        out[f"aux_mask{v}"] = np.ascontiguousarray(masks[v])
    return out


L2A_PHASES = 31

def build_L2_attn():
    nc, p, xin, xout, fout, aux = new_prog()
    wqd = din(nc, "wq", [D, D]); wkd = din(nc, "wk", [D, D])
    wvd = din(nc, "wv", [D, D]); wod = din(nc, "wo", [D, D])
    qnw = din(nc, "qnw", [64]); knw = din(nc, "knw", [64])
    cosd = din(nc, "aux_cos", [128, T], BF16)
    sind = din(nc, "aux_sin", [128, T], BF16)
    permd = din(nc, "aux_perm", [128, 128], BF16)
    eseld = din(nc, "aux_esel", [128, 2])
    bseld = din(nc, "aux_bsel", [2, 128], BF16)
    masksd = [din(nc, f"aux_mask{v}", [128, PAN]) for v in range(4)]
    NH, DH = 16, 64
    EPS = 1e-6

    cosb = p.sb("cosb", [128, T], BF16); p.dma(cosb[:], cosd)
    sinb = p.sb("sinb", [128, T], BF16); p.dma(sinb[:], sind)
    permb = p.sb("permb", [128, 128], BF16); p.dma(permb[:], permd)
    eselb = p.sb("eselb", [128, 2], F32); p.dma(eselb[:], eseld)
    bselb = p.sb("bselb", [2, 128], BF16); p.dma(bselb[:], bseld)
    masksb = [p.sb(f"mask{v}", [128, PAN], F32) for v in range(4)]
    for v in range(4):
        p.dma(masksb[v][:], masksd[v])

    wqt = p.sb("wqt", [128, 1], F32)
    wkt = p.sb("wkt", [128, 1], F32)
    for half in range(2):
        p.dma(wqt[half * 64:(half + 1) * 64, :], qnw.rearrange("(q one) -> q one", one=1))
        p.dma(wkt[half * 64:(half + 1) * 64, :], knw.rearrange("(q one) -> q one", one=1))
    epsb = p.sb("epsb", [2, 1], F32)
    p.v("memset", epsb[:], EPS)

    for sq in range(NSEQ):
        base = sq * T
        with p.scope():
            qro = [p.sb(f"qro{c}", [128, T], BF16) for c in range(NCH)]
            kro = [p.sb(f"kro{c}", [128, T], BF16) for c in range(NCH)]
            vtm = [p.sb(f"vtm{i}", [128, D], BF16) for i in range(T // 128)]
            with p.scope():
                xbf = load_x_bf(p, xin, T, base)
                ws = WStrip(p, NCH, 128, "ws")
                wv5 = WStrip(p, NCH, 512, "wv5")
                pA = p.ps("pA", [128, PAN])
                pH = p.ps("pH", [2, PAN])
                pB = p.ps("pB", [128, PAN])
                pW = p.ps("pW", [128, PAN])
                sq_sb = p.sb("sq_sb", [128, PAN], F32)
                qraw = p.sb("qraw", [128, PAN], F32)
                rst = p.sb("rst", [2, PAN], F32)
                rstb = p.sb("rstb", [2, PAN], BF16)
                qn = p.sb("qn", [128, PAN], F32)
                qnb = p.sb("qnb", [128, PAN], BF16)
                r1 = p.sb("r1", [128, PAN], F32)
                r2 = p.sb("r2", [128, PAN], F32)
                for dst, wd, gaint in (((qro, wqd, wqt), (kro, wkd, wkt)) if L2A_PHASES & 8 else ()):
                    for m in range(NCH):
                        ws.load(wd, m * 128)
                        for n in range(T // PAN):
                            c0 = n * PAN
                            mm_chunks(p, pA[:], ws, xbf, c0, PAN, mtile=(0, 128))
                            p.v("tensor_copy", qraw[:], pA[:])
                            p.s("square", sq_sb[:], qraw[:])
                            p.t("matmul", pH[:], eselb[:], sq_sb[:], start=True, stop=True)
                            p.v("tensor_scalar_mul", rst[:], pH[:], 1.0 / DH)
                            p.s("activation", rst[:], rst[:], AF.Ln, bias=epsb[:])
                            p.s("activation", rst[:], rst[:], AF.Exp, scale=-0.5)
                            p.v("tensor_copy", rstb[:], rst[:])
                            p.t("matmul", pB[:], bselb[:], rstb[:], start=True, stop=True)
                            p.v("tensor_mul", qn[:], qraw[:], pB[:])
                            p.v("tensor_scalar_mul", qn[:], qn[:], gaint[:])
                            p.v("tensor_copy", qnb[:], qn[:])
                            p.t("matmul", pW[:], permb[:], qnb[:], start=True, stop=True)
                            p.v("tensor_mul", r1[:], qn[:], cosb[:, c0:c0 + PAN])
                            p.v("tensor_mul", r2[:], pW[:], sinb[:, c0:c0 + PAN])
                            p.v("tensor_add", r1[:], r1[:], r2[:])
                            p.v("tensor_copy", dst[m][:, c0:c0 + PAN], r1[:])
                for i in range(T // 128 if L2A_PHASES & 16 else 0):
                    for n2 in range(2):
                        wv5.load(wvd, n2 * 512)
                        for kc in range(NCH):
                            p.t("matmul", pA[:], xbf[kc][:, i * 128:(i + 1) * 128],
                                wv5.tile(kc, 0, 512), start=(kc == 0), stop=(kc == NCH - 1))
                        p.v("tensor_copy", vtm[i][:, n2 * 512:(n2 + 1) * 512], pA[:])
            yfm = [p.sb(f"yfm{c}", [128, T], BF16) for c in range(NCH)]
            with p.scope():
                pS = p.ps("pS", [128, PAN])
                pOkb = p.ps("pOkb", [128, DH])
                pTr = p.ps("pTr", [128, 128], BF16)
                pYT = p.ps("pYT", [64, 128], BF16)
                s_sb = p.sb("s_sb", [128, PAN], F32)
                pbf = p.sb("pbf", [128, PAN], BF16)
                ptr_sb = p.sb("ptr_sb", [128, 512], BF16)
                o_acc = p.sb("o_acc", [128, DH], F32)
                mrun = p.sb("mrun", [128, 1], F32)
                mblk = p.sb("mblk", [128, 1], F32)
                negm = p.sb("negm", [128, 1], F32)
                lsum = p.sb("lsum", [128, 1], F32)
                lblk = p.sb("lblk", [128, 1], F32)
                yq = p.sb("yq", [128, DH], F32)
                yqb = p.sb("yqb", [128, DH], BF16)
                for h in range(NH if L2A_PHASES & 2 else 0):
                    qh = qro[h // 2][(h % 2) * 64:(h % 2) * 64 + 64, :]
                    kh = kro[h // 2][(h % 2) * 64:(h % 2) * 64 + 64, :]
                    for qt in range(T // 128):
                        nkb = qt // 4 + 1
                        p.v("memset", mrun[:], -1e30)
                        for kb in range(nkb):
                            p.t("matmul", pS[:], qh[:, qt * 128:(qt + 1) * 128],
                                kh[:, kb * PAN:(kb + 1) * PAN], start=True, stop=True)
                            if kb == qt // 4:
                                p.v("tensor_add", s_sb[:], pS[:], masksb[qt % 4][:])
                                p.v("reduce_max", mblk[:], s_sb[:], axis=AX.X)
                            else:
                                p.v("reduce_max", mblk[:], pS[:], axis=AX.X)
                            p.v("tensor_max", mrun[:], mrun[:], mblk[:])
                        p.v("tensor_scalar_mul", negm[:], mrun[:], -0.125)
                        p.v("memset", lsum[:], 0.0)
                        p.v("memset", o_acc[:], 0.0)
                        for kb in range(nkb if L2A_PHASES & 4 else 0):
                            p.t("matmul", pS[:], qh[:, qt * 128:(qt + 1) * 128],
                                kh[:, kb * PAN:(kb + 1) * PAN], start=True, stop=True)
                            if kb == qt // 4:
                                p.v("tensor_add", s_sb[:], pS[:], masksb[qt % 4][:])
                                p.s("activation", pbf[:], s_sb[:], AF.Exp,
                                    bias=negm[:], scale=0.125)
                            else:
                                p.s("activation", pbf[:], pS[:], AF.Exp,
                                    bias=negm[:], scale=0.125)
                            p.v("reduce_sum", lblk[:], pbf[:], axis=AX.X)
                            p.v("tensor_add", lsum[:], lsum[:], lblk[:])
                            nj = min(4, qt - kb * 4 + 1)
                            for j in range(nj):
                                p.t("transpose", pTr[:], pbf[:, j * 128:(j + 1) * 128],
                                    aux["identity"][:])
                                p.v("tensor_copy", ptr_sb[:, j * 128:(j + 1) * 128], pTr[:])
                            for j in range(nj):
                                p.t("matmul", pOkb[:], ptr_sb[:, j * 128:(j + 1) * 128],
                                    vtm[kb * 4 + j][:, h * 64:(h + 1) * 64],
                                    start=(j == 0), stop=(j == nj - 1))
                            p.v("tensor_add", o_acc[:], o_acc[:], pOkb[:])
                        p.v("reciprocal", lblk[:], lsum[:])
                        p.v("tensor_scalar_mul", yq[:], o_acc[:], lblk[:])
                        p.v("tensor_copy", yqb[:], yq[:])
                        p.t("transpose", pYT[:], yqb[:], aux["identity"][0:128, 0:128])
                        p.v("tensor_copy",
                            yfm[h // 2][(h % 2) * 64:(h % 2) * 64 + 64,
                                        qt * 128:(qt + 1) * 128], pYT[:])
            with p.scope():
                wos = WStrip(p, NCH, 128, "wos")
                pA2 = p.ps("pA2", [128, PAN])
                t512 = p.sb("t512", [128, PAN], F32)
                for m in range(NCH):
                    wos.load(wod, m * 128)
                    for n in range(T // PAN):
                        mm_chunks(p, pA2[:], wos, yfm, n * PAN, PAN, mtile=(0, 128))
                        p.v("tensor_copy", t512[:], pA2[:])
                        p.dma(fout[m * 128:(m + 1) * 128, base + n * PAN: base + (n + 1) * PAN],
                              t512[:])
    ln_epilogue(p, xin, fout, xout, aux)
    p.finalize()
    return nc

class QuantScratch:
    def __init__(self, p, Fmax):
        B = Fmax // 16
        self.sc = p.sb("q_sc", [128, B], F32)
        self.rec = p.sb("q_rec", [128, B], F32)
        self.exp = p.sb("q_exp", [128, Fmax], F32)
        self.a = p.sb("q_a", [128, Fmax], F32)
        self.qv = p.sb("q_qv", [128, Fmax], F32)
        self.qb = p.sb("q_qb", [128, Fmax], F32)
        self.sg = self.a


def emit_fp4_quant(p, dst_bf, src, F, qs):
    B = F // 16
    sc = qs.sc[:, 0:B]; rec = qs.rec[:, 0:B]
    exp = qs.exp[:, 0:F]; a = qs.a[:, 0:F]
    qv = qs.qv[:, 0:F]; qb = qs.qb[:, 0:F]; sg = qs.sg[:, 0:F]
    sv = lambda t: t.rearrange("p (b i) -> p b i", i=16)
    p.v("reduce_max", sc.rearrange("p (b one) -> p b one", one=1), sv(src),
        axis=AX.X, apply_absolute_value=True)
    p.v("tensor_scalar_mul", sc, sc, 1.0 / 6.0)
    p.v("tensor_scalar_max", sc, sc, 1e-12)
    p.v("reciprocal", rec, sc)
    for j in range(16):
        p.v("tensor_copy", sv(exp)[:, :, j], rec)
    p.s("activation", a, src, AF.Abs)
    p.v("tensor_mul", a, a, exp)
    p.v("tensor_single_scalar", qv, a, 0.25, ALU.is_gt)
    p.v("scalar_tensor_tensor", qb, a, 0.75, qv, ALU.is_gt, ALU.add)
    p.v("scalar_tensor_tensor", qv, a, 1.25, qb, ALU.is_gt, ALU.add)
    p.v("scalar_tensor_tensor", qb, a, 1.75, qv, ALU.is_gt, ALU.add)
    p.v("tensor_scalar_mul", qv, qb, 0.5)
    p.v("scalar_tensor_tensor", qb, a, 2.5, qv, ALU.is_gt, ALU.add)
    p.v("scalar_tensor_tensor", qv, a, 3.5, qb, ALU.is_gt, ALU.add)
    p.v("tensor_scalar", qb, a, 5.0, 2.0, ALU.is_gt, ALU.mult)
    p.v("tensor_add", qv, qv, qb)
    p.s("sign", sg, src)
    p.v("tensor_mul", qv, qv, sg)
    for j in range(16):
        p.v("tensor_copy", sv(exp)[:, :, j], sc)
    p.v("tensor_mul", dst_bf, qv, exp)



def build_L2_ffn():
    nc, p, xin, xout, fout, aux = new_prog()
    wgate = din(nc, "wgate", [2816, D])
    wup = din(nc, "wup", [2816, D])
    wdown = din(nc, "wdown", [D, 2816])
    I = 2816
    IC = I // 128
    wgT = nc.dram_tensor("wgT", [D, I], BF16, kind="Internal").ap()
    wuT = nc.dram_tensor("wuT", [D, I], BF16, kind="Internal").ap()
    wdT = nc.dram_tensor("wdT", [I, D], BF16, kind="Internal").ap()

    with p.scope():
        qs = QuantScratch(p, I)
        stg = p.sb("stg", [128, I], F32)
        qbf = p.sb("qbf", [128, I], BF16)
        tb = p.sb("tb", [128, 128], BF16)
        pT = p.ps("pT", [128, 128], BF16)
        for wsrc, wdst, rows, cols in ((wgate, wgT, IC, 8), (wup, wuT, IC, 8),
                                       (wdown, wdT, 8, IC)):
            F = cols * 128
            for r in range(rows):
                p.dma(stg[:, 0:F], wsrc[r * 128:(r + 1) * 128, 0:F])
                emit_fp4_quant(p, qbf[:, 0:F], stg[:, 0:F], F, qs)
                for kc in range(cols):
                    p.t("transpose", pT[:], qbf[:, kc * 128:(kc + 1) * 128],
                        aux["identity"][:])
                    p.v("tensor_copy", tb[:], pT[:])
                    p.dma(wdst[kc * 128:(kc + 1) * 128, r * 128:(r + 1) * 128], tb[:])
    xq = [p.sb(f"xq{c}", [128, NTOK], BF16) for c in range(NCH)]
    with p.scope():
        xbf = load_x_bf(p, xin)
        qs = QuantScratch(p, D)
        x_tm = p.sb("x_tm", [128, D], BF16)
        xq_tm = p.sb("xq_tm", [128, D], BF16)
        tb = p.sb("tb2", [128, 128], BF16)
        pT = p.ps("pT2", [128, 128], BF16)
        for i in range(NTOK // 128):
            for kc in range(NCH):
                p.t("transpose", pT[:], xbf[kc][:, i * 128:(i + 1) * 128],
                    aux["identity"][:])
                p.v("tensor_copy", x_tm[:, kc * 128:(kc + 1) * 128], pT[:])
            emit_fp4_quant(p, xq_tm[:], x_tm[:], D, qs)
            for kc in range(NCH):
                p.t("transpose", pT[:], xq_tm[:, kc * 128:(kc + 1) * 128],
                    aux["identity"][:])
                p.v("tensor_copy", xq[kc][:, i * 128:(i + 1) * 128], pT[:])
    with p.scope():
        qs = QuantScratch(p, I)
        h_bf = [p.sb(f"hbf{m}", [128, PAN], BF16) for m in range(IC)]
        hq = [p.sb(f"hq{m}", [128, PAN], BF16) for m in range(IC)]
        h_tm = p.sb("h_tm", [128, I], BF16)
        hq_tm = p.sb("hq_tm", [128, I], BF16)
        wsg = WStrip(p, NCH, 128, "wsg", with_stage=False)
        wsu = WStrip(p, NCH, 128, "wsu", with_stage=False)
        wsd = WStrip(p, IC, 128, "wsd", with_stage=False)
        pG = p.ps("pG", [128, PAN])
        pU = p.ps("pU", [128, PAN])
        pT3 = p.ps("pT3", [128, 128], BF16)
        sG = p.sb("sG", [128, PAN], F32)
        t512 = p.sb("t512f", [128, PAN], F32)
        for pan in range(NTOK // PAN):
            c0 = pan * PAN
            for m in range(IC):
                p.dma(wsg.buf[:].rearrange("q (kc m) -> q kc m", m=128),
                      wgT[:, m * 128:(m + 1) * 128]
                      .rearrange("(kc q) m -> q kc m", q=128))
                p.dma(wsu.buf[:].rearrange("q (kc m) -> q kc m", m=128),
                      wuT[:, m * 128:(m + 1) * 128]
                      .rearrange("(kc q) m -> q kc m", q=128))
                wsg.cur_mw = 128; wsu.cur_mw = 128
                mm_chunks(p, pG[:], wsg, xq, c0, PAN, mtile=(0, 128))
                mm_chunks(p, pU[:], wsu, xq, c0, PAN, mtile=(0, 128))
                p.s("activation", sG[:], pG[:], AF.Silu)
                p.v("tensor_mul", h_bf[m][:], sG[:], pU[:])
            for tc in range(PAN // 128):
                for m in range(IC):
                    p.t("transpose", pT3[:], h_bf[m][:, tc * 128:(tc + 1) * 128],
                        aux["identity"][:])
                    p.v("tensor_copy", h_tm[:, m * 128:(m + 1) * 128], pT3[:])
                emit_fp4_quant(p, hq_tm[:], h_tm[:], I, qs)
                for m in range(IC):
                    p.t("transpose", pT3[:], hq_tm[:, m * 128:(m + 1) * 128],
                        aux["identity"][:])
                    p.v("tensor_copy", hq[m][:, tc * 128:(tc + 1) * 128], pT3[:])
            for m2 in range(NCH):
                p.dma(wsd.buf[:].rearrange("q (kc m) -> q kc m", m=128),
                      wdT[:, m2 * 128:(m2 + 1) * 128]
                      .rearrange("(kc q) m -> q kc m", q=128))
                wsd.cur_mw = 128
                for i, kc in enumerate(range(IC)):
                    p.t("matmul", pG[:], wsd.tile(kc, 0, 128), hq[kc][:],
                        start=(i == 0), stop=(i == IC - 1))
                p.v("tensor_copy", t512[:], pG[:])
                p.dma(fout[m2 * 128:(m2 + 1) * 128, c0:c0 + PAN], t512[:])
    ln_epilogue(p, xin, fout, xout, aux)
    p.finalize()
    return nc



def l3a_aux():
    hsel = np.zeros((16, D), np.float32)
    for h in range(16):
        hsel[h, h * 64:(h + 1) * 64] = 1.0
    return {"aux_hsel": hsel.astype(BF)}


def build_L3_attn():
    nc, p, xin, xout, fout, aux = new_prog()
    wqT = din(nc, "wqT", [D, 16])
    wkT = din(nc, "wkT", [D, 16])
    wvP = din(nc, "wvP", [D, D])
    woP = din(nc, "woP", [D, D])
    hseld = din(nc, "aux_hsel", [16, D], BF16)
    PI = float(np.pi)
    EPS = 1e-6

    hselb = p.sb("hselb", [16, D], BF16)
    p.dma(hselb[:], hseld)
    pio2 = p.sb("pio2", [16, 1], F32)
    p.v("memset", pio2[:], PI / 2.0)

    for sq in range(NSEQ):
        base = sq * T
        with p.scope():
            vbf = [p.sb(f"vbf{m}", [128, T], BF16) for m in range(NCH)]
            heads_bf = [p.sb(f"hd{m}", [128, T], BF16) for m in range(NCH)]
            cq = p.sb("cq", [16, T], BF16)
            sqb = p.sb("sq", [16, T], BF16)
            ck = p.sb("ck", [16, T], BF16)
            sk = p.sb("sk", [16, T], BF16)
            rec_bf = p.sb("recbf", [16, T], BF16)
            NPS = T // PAN
            with p.scope():
                xbf = load_x_bf(p, xin, T, base)
                wqs = WStrip(p, NCH, 16, "wqs")
                wvs = WStrip(p, NCH, 128, "wvs")
                pC = p.ps("pC", [16, PAN])
                pA0 = p.ps("pA0", [128, PAN])
                tq = p.sb("tqa", [16, T], F32)
                tk = p.sb("tka", [16, T], F32)
                ckf = p.sb("ckf", [16, T], F32)
                skf = p.sb("skf", [16, T], F32)
                den = p.sb("den", [16, T], F32)
                zer16 = p.sb("zer16", [16, T], F32)
                tmp16 = p.sb("tmp16", [16, T], F32)
                p.v("memset", zer16[:], 0.0)
                for dstt, dcos, dsin, wmat in ((tq, cq, sqb, wqT), (tk, ck, sk, wkT)):
                    wqs.load(wmat, 0)
                    for n in range(NPS):
                        mm_chunks(p, pC[:], wqs, xbf, n * PAN, PAN, mtile=(0, 16))
                        p.s("activation", dstt[:, n * PAN:(n + 1) * PAN], pC[:], AF.Tanh)
                    p.s("activation", dcos[:], dstt[:], AF.Sin, scale=PI / 4.0, bias=pio2[:])
                    p.s("activation", dsin[:], dstt[:], AF.Sin, scale=PI / 4.0)
                for m in range(NCH):
                    wvs.load(wvP, m * 128)
                    for n in range(NPS):
                        mm_chunks(p, pA0[:], wvs, xbf, n * PAN, PAN, mtile=(0, 128))
                        p.v("tensor_copy", vbf[m][:, n * PAN:(n + 1) * PAN], pA0[:])
                p.v("tensor_copy", ckf[:], ck[:])
                p.v("tensor_copy", skf[:], sk[:])
                p.v("tensor_tensor_scan", ckf[:], ckf[:], zer16[:], 0.0, ALU.add, ALU.add)
                p.v("tensor_tensor_scan", skf[:], skf[:], zer16[:], 0.0, ALU.add, ALU.add)
                p.v("tensor_mul", den[:], ckf[:], cq[:])
                p.v("tensor_mul", tmp16[:], skf[:], sqb[:])
                p.v("tensor_add", den[:], den[:], tmp16[:])
                p.v("tensor_scalar_add", den[:], den[:], EPS)
                p.v("reciprocal", den[:], den[:])
                p.v("tensor_copy", rec_bf[:], den[:])
            pA = p.ps("pA", [128, PAN])
            with p.scope():
                PC = p.sb("PC", [128, T], F32)
                SC = p.sb("SC", [128, T], F32)
                SP = p.sb("SP", [128, T], F32)
                hm = p.sb("hm", [128, T], F32)
                zer = p.sb("zerbig", [128, T], F32)
                p.v("memset", zer[:], 0.0)
                for m in range(NCH):
                    hs = hselb[:, m * 128:(m + 1) * 128]
                    for src_trig, scan_dst in ((ck, SC), (sk, SP)):
                        for n in range(NPS):
                            p.t("matmul", pA[:], hs, src_trig[:, n * PAN:(n + 1) * PAN],
                                start=True, stop=True)
                            p.v("tensor_mul", PC[:, n * PAN:(n + 1) * PAN], pA[:],
                                vbf[m][:, n * PAN:(n + 1) * PAN])
                        p.v("tensor_tensor_scan", scan_dst[:], PC[:], zer[:], 0.0,
                            ALU.add, ALU.add)
                    for n in range(NPS):
                        p.t("matmul", pA[:], hs, cq[:, n * PAN:(n + 1) * PAN],
                            start=True, stop=True)
                        p.v("tensor_mul", hm[:, n * PAN:(n + 1) * PAN], pA[:],
                            SC[:, n * PAN:(n + 1) * PAN])
                    for n in range(NPS):
                        p.t("matmul", pA[:], hs, sqb[:, n * PAN:(n + 1) * PAN],
                            start=True, stop=True)
                        p.v("tensor_mul", PC[:, n * PAN:(n + 1) * PAN], pA[:],
                            SP[:, n * PAN:(n + 1) * PAN])
                    p.v("tensor_add", hm[:], hm[:], PC[:])
                    for n in range(NPS):
                        p.t("matmul", pA[:], hs, rec_bf[:, n * PAN:(n + 1) * PAN],
                            start=True, stop=True)
                        p.v("tensor_mul", heads_bf[m][:, n * PAN:(n + 1) * PAN],
                            hm[:, n * PAN:(n + 1) * PAN], pA[:])
            with p.scope():
                wos = WStrip(p, NCH, 128, "wos")
                t512 = p.sb("t512a", [128, PAN], F32)
                for m in range(NCH):
                    wos.load(woP, m * 128)
                    for n in range(NPS):
                        mm_chunks(p, pA[:], wos, heads_bf, n * PAN, PAN, mtile=(0, 128))
                        p.v("tensor_copy", t512[:], pA[:])
                        p.dma(fout[m * 128:(m + 1) * 128,
                                   base + n * PAN: base + (n + 1) * PAN], t512[:])
    ln_epilogue(p, xin, fout, xout, aux)
    p.finalize()
    return nc



def build_L3_ffn():
    nc, p, xin, xout, fout, aux = new_prog()
    gatew = din(nc, "gatew", [D, 8]); gateb = din(nc, "gateb", [8])
    upw = din(nc, "upw", [8, D, 4096]); upb = din(nc, "upb", [8, 4096])
    downw = din(nc, "downw", [8, 4096, D]); downb = din(nc, "downb", [8, D])
    swup = din(nc, "swup", [D, 4096]); sbup = din(nc, "sbup", [4096])
    swdown = din(nc, "swdown", [4096, D]); sbdown = din(nc, "sbdown", [D])
    IC32 = 32
    acc = nc.dram_tensor("accd", [D, NTOK], F32, kind="Internal").ap()
    w_scr = nc.dram_tensor("wscr", [8, NTOK], BF16, kind="Internal").ap()
    upw_bf = nc.dram_tensor("upwbf", [8, D, 4096], BF16, kind="Internal").ap()
    dnw_bf = nc.dram_tensor("dnwbf", [8, 4096, D], BF16, kind="Internal").ap()

    xbf = load_x_bf(p, xin)
    with p.scope():
        cst = p.sb("cst", [128, 4096], F32)
        cbf = p.sb("cbf", [128, 4096], BF16)
        for e in range(8):
            for r in range(NCH):
                p.dma(cst[:], upw[e, r * 128:(r + 1) * 128, :])
                p.v("tensor_copy", cbf[:], cst[:])
                p.dma(upw_bf[e, r * 128:(r + 1) * 128, :], cbf[:])
            for r in range(IC32):
                p.dma(cst[:, 0:D], downw[e, r * 128:(r + 1) * 128, :])
                p.v("tensor_copy", cbf[:, 0:D], cst[:, 0:D])
                p.dma(dnw_bf[e, r * 128:(r + 1) * 128, :], cbf[:, 0:D])
    with p.scope():
        gws = WStrip(p, NCH, 8, "gws")
        gws.load(gatew, 0)
        gb_st = p.sb("gbst", [1, 8], F32)
        gb_bf = p.sb("gbbf", [1, 8], BF16)
        p.dma(gb_st[:], gateb.rearrange("(one n) -> one n", one=1))
        p.v("tensor_copy", gb_bf[:], gb_st[:])
        pG8 = p.ps("pG8", [128, 8])
        pW8 = p.ps("pW8", [8, 128], BF16)
        s_tm = p.sb("s_tm", [128, 8], F32)
        ind = p.sb("ind", [128, 8], F32)
        msk = p.sb("msk", [128, 8], F32)
        m1 = p.sb("m1", [128, 1], F32)
        m2 = p.sb("m2", [128, 1], F32)
        rec = p.sb("rec", [128, 1], F32)
        wbf = p.sb("wbf", [128, 8], BF16)
        w8sb = p.sb("w8sb", [8, 128], BF16)
        for i in range(NTOK // 128):
            p.t("matmul", pG8[:], aux["ones1_bf"][:], gb_bf[:], start=True, stop=False)
            for kc in range(NCH):
                p.t("matmul", pG8[:], xbf[kc][:, i * 128:(i + 1) * 128],
                    gws.tile(kc, 0, 8), start=False, stop=(kc == NCH - 1))
            p.s("activation", s_tm[:], pG8[:], AF.Sigmoid)
            p.v("reduce_max", m1[:], s_tm[:], axis=AX.X)
            p.v("tensor_single_scalar", ind[:], s_tm[:], m1[:], ALU.is_ge)
            p.v("scalar_tensor_tensor", msk[:], ind[:], -1e30, s_tm[:], ALU.mult, ALU.add)
            p.v("reduce_max", m2[:], msk[:], axis=AX.X)
            p.v("tensor_single_scalar", ind[:], s_tm[:], m2[:], ALU.is_ge)
            p.v("tensor_add", rec[:], m1[:], m2[:])
            p.v("tensor_scalar_add", rec[:], rec[:], 1e-6)
            p.v("reciprocal", rec[:], rec[:])
            p.v("tensor_mul", ind[:], ind[:], s_tm[:])
            p.v("tensor_scalar", wbf[:], ind[:], rec[:], 0.1, ALU.mult, ALU.mult)
            p.t("transpose", pW8[:], wbf[:], aux["identity"][:])
            p.v("tensor_copy", w8sb[:], pW8[:])
            p.dma(w_scr[:, i * 128:(i + 1) * 128], w8sb[:])
    with p.scope():
        sbup_t = col_tiles(p, sbup, 4096, "sbupt")
        sbdn_t = col_tiles(p, sbdown, D, "sbdnt")
        wsu = WStrip(p, NCH, 128, "wsu3")
        wsd = WStrip(p, IC32, 128, "wsd3")
        g_bf = [p.sb(f"g{m}", [128, PAN], BF16) for m in range(IC32)]
        pA = p.ps("pA3", [128, PAN])
        t512 = p.sb("t5123", [128, PAN], F32)
        for pan in range(NTOK // PAN):
            c0 = pan * PAN
            for m in range(IC32):
                wsu.load(swup, m * 128)
                mm_chunks(p, pA[:], wsu, xbf, c0, PAN, mtile=(0, 128))
                p.s("activation", g_bf[m][:], pA[:], AF.Gelu, bias=sbup_t[m][:])
            for m2 in range(NCH):
                wsd.load(swdown, m2 * 128)
                mm_chunks(p, pA[:], wsd, g_bf, 0, PAN, mtile=(0, 128))
                p.v("tensor_scalar", t512[:], pA[:], sbdn_t[m2][:], 0.1, ALU.add, ALU.mult)
                p.dma(acc[m2 * 128:(m2 + 1) * 128, c0:c0 + PAN], t512[:])
    with p.scope():
        TQ = 1024
        he = [p.sb(f"he{m}", [128, TQ], BF16) for m in range(IC32)]
        wstr = p.sb("wstr", [128, IC32 * 128], BF16)
        pA = p.ps("pA4", [128, PAN])
        pWb = p.ps("pWb", [128, PAN])
        oe = p.sb("oe", [128, PAN], F32)
        at = p.sb("at", [128, PAN], F32)
        wrow = p.sb("wrow", [1, PAN], BF16)
        for e in range(8):
            upb_t = None
            with p.scope():
                upb_t = col_tiles(p, upb[e, :], 4096, f"upbt{e}")
                dnb_t = col_tiles(p, downb[e, :], D, f"dnbt{e}")
                for qr in range(NTOK // TQ):
                    c0 = qr * TQ
                    for m in range(IC32):
                        p.dma(wstr[:, 0:NCH * 128].rearrange("q (kc m) -> q kc m", m=128),
                              upw_bf[e, :, m * 128:(m + 1) * 128]
                              .rearrange("(kc q) m -> q kc m", q=128))
                        for pan2 in range(TQ // PAN):
                            cc = c0 + pan2 * PAN
                            for kc in range(NCH):
                                p.t("matmul", pA[:], wstr[:, kc * 128:(kc + 1) * 128],
                                    xbf[kc][:, cc:cc + PAN],
                                    start=(kc == 0), stop=(kc == NCH - 1))
                            p.s("activation", he[m][:, pan2 * PAN:(pan2 + 1) * PAN],
                                pA[:], AF.Gelu, bias=upb_t[m][:])
                    for pan2 in range(TQ // PAN):
                        cc = c0 + pan2 * PAN
                        p.dma(wrow[:], w_scr[e:e + 1, cc:cc + PAN])
                        p.t("matmul", pWb[:], aux["ones1_bf"][:], wrow[:],
                            start=True, stop=True)
                        for m2 in range(NCH):
                            p.dma(wstr[:, 0:IC32 * 128].rearrange("q (kc m) -> q kc m", m=128),
                                  dnw_bf[e, :, m2 * 128:(m2 + 1) * 128]
                                  .rearrange("(kc q) m -> q kc m", q=128))
                            for kc in range(IC32):
                                p.t("matmul", pA[:], wstr[:, kc * 128:(kc + 1) * 128],
                                    he[kc][:, pan2 * PAN:(pan2 + 1) * PAN],
                                    start=(kc == 0), stop=(kc == IC32 - 1))
                            p.v("tensor_scalar_add", oe[:], pA[:], dnb_t[m2][:])
                            p.v("tensor_mul", oe[:], oe[:], pWb[:])
                            p.dma(at[:], acc[m2 * 128:(m2 + 1) * 128, cc:cc + PAN])
                            p.v("tensor_add", at[:], at[:], oe[:])
                            p.dma(acc[m2 * 128:(m2 + 1) * 128, cc:cc + PAN], at[:])
    with p.scope():
        M1 = p.sb("M1", [128, NTOK], F32)
        ch = p.sb("ch", [128, NTOK], F32)
        M1b = p.sb("M1b", [128, 128], BF16)
        mcol = p.sb("mcol", [128, IC32], F32)
        mrow = p.sb("mrow", [1, NTOK], F32)
        mrow_bf = p.sb("mrowld", [1, NTOK], BF16)
        mrowb = p.sb("mrowb", [1, NTOK], BF16)
        pT = p.ps("pT5", [128, 128], BF16)
        pM = p.ps("pM", [IC32, 128], BF16)
        pB = p.ps("pB5", [128, PAN])
        mtmp = p.sb("mtmp", [128, 1], F32)
        m_scr = nc.dram_tensor("mscr", [IC32, 128], BF16, kind="Internal").ap()
        mc_bf = p.sb("mcbf", [128, IC32], BF16)
        m32sb = p.sb("m32sb", [IC32, 128], BF16)
        t512 = p.sb("t5125", [128, PAN], F32)
        for c in range(NCH):
            p.dma(ch[:], acc[c * 128:(c + 1) * 128, :])
            if c == 0:
                p.s("activation", M1[:], ch[:], AF.Abs)
            else:
                p.s("activation", ch[:], ch[:], AF.Abs)
                p.v("tensor_max", M1[:], M1[:], ch[:])
        for i in range(NTOK // 128):
            p.v("tensor_copy", M1b[:], M1[:, i * 128:(i + 1) * 128])
            p.t("transpose", pT[:], M1b[:], aux["identity"][:])
            p.v("reduce_max", mtmp[:], pT[:], axis=AX.X)
            p.v("tensor_copy", mcol[:, i:i + 1], mtmp[:])
        p.v("tensor_copy", mc_bf[:], mcol[:])
        p.t("transpose", pM[:], mc_bf[:], aux["identity"][:])
        p.v("tensor_copy", m32sb[:], pM[:])
        p.dma(m_scr, m32sb[:])
        p.dma(mrow_bf[:], m_scr.rearrange("a b -> (a b)").rearrange("(one n) -> one n", one=1))
        p.v("tensor_copy", mrow[:], mrow_bf[:])
        p.v("tensor_scalar_add", mrow[:], mrow[:], 1e-6)
        p.v("reciprocal", mrow[:], mrow[:])
        p.v("tensor_copy", mrowb[:], mrow[:])
        for pan in range(NTOK // PAN):
            c0 = pan * PAN
            p.t("matmul", pB[:], aux["ones1_bf"][:], mrowb[0:1, c0:c0 + PAN],
                start=True, stop=True)
            for c in range(NCH):
                p.dma(ch[:, 0:PAN], acc[c * 128:(c + 1) * 128, c0:c0 + PAN])
                p.v("tensor_mul", t512[:], ch[:, 0:PAN], pB[:])
                p.dma(fout[c * 128:(c + 1) * 128, c0:c0 + PAN], t512[:])
    ln_epilogue(p, xin, fout, xout, aux)
    p.finalize()
    return nc



def shard_x(x):
    out = []
    for c in range(NCORES):
        xc = x[2 * c:2 * c + 2].reshape(NTOK, D)
        out.append(np.ascontiguousarray(xc.T))
    return out


def unshard_x(shards):
    full = np.empty((16, T, D), np.float32)
    for c in range(NCORES):
        full[2 * c:2 * c + 2] = np.asarray(shards[c], np.float32).T.reshape(2, T, D)
    return full


def run_block(nc, xin_shards, extra, out_name="xout"):
    base = dict(extra)
    base.update(aux_arrays())
    in_maps = [dict(base, xin=xin_shards[c]) for c in range(NCORES)]
    res = run_bass_kernel_spmd(nc, in_maps, core_ids=list(range(NCORES)))
    return [np.ascontiguousarray(r[out_name]) for r in res.results]


def kernel(**inp):
    inp = {k: np.asarray(v) for k, v in inp.items()}
    xs = shard_x(np.ascontiguousarray(inp["L0_attn_x"], np.float32))
    xs = run_block(build_L0_attn(), xs, {
        "gw1": inp["L0_attn_gate_w1"], "gb1": inp["L0_attn_gate_b1"],
        "gw2": inp["L0_attn_gate_w2"], "gb2": inp["L0_attn_gate_b2"],
        "uw": inp["L0_attn_U_w"], "vw": inp["L0_attn_V_w"],
        "convw": np.ascontiguousarray(inp["L0_attn_conv_w"].T),
        "outw": inp["L0_attn_out_w"], "outb": inp["L0_attn_out_b"]})
    xs = run_block(build_L0_ffn(float(inp["L0_ffn_s0"]), float(inp["L0_ffn_s1"]),
                                float(inp["L0_ffn_s2"])), xs, {
        "k0": inp["L0_ffn_k0"].astype(np.uint8),
        "k1": inp["L0_ffn_k1"].astype(np.uint8),
        "k2": inp["L0_ffn_k2"].astype(np.uint8)})
    xs = run_block(build_L1_attn(), xs, {
        "wqkv": inp["L1_attn_W_attn"], "battn": inp["L1_attn_b_attn"],
        "wproj": inp["L1_attn_W_proj"], "bproj": inp["L1_attn_b_proj"],
        **l1a_aux()})
    xs = run_block(build_L1_ffn(), xs, {
        "wrT": np.ascontiguousarray(inp["L1_ffn_W_r"].T),
        "wkT": np.ascontiguousarray(inp["L1_ffn_W_k"].T),
        "wvT": np.ascontiguousarray(inp["L1_ffn_W_v"].T),
        "mur": inp["L1_ffn_mu_r"], "muk": inp["L1_ffn_mu_k"]})
    xs = run_block(build_L2_attn(), xs, {
        "wq": inp["L2_attn_Wq"], "wk": inp["L2_attn_Wk"],
        "wv": inp["L2_attn_Wv"], "wo": inp["L2_attn_Wo"],
        "qnw": inp["L2_attn_q_norm_w"], "knw": inp["L2_attn_k_norm_w"],
        **l2a_aux()})
    xs = run_block(build_L2_ffn(), xs, {
        "wgate": inp["L2_ffn_w_gate"], "wup": inp["L2_ffn_w_up"],
        "wdown": inp["L2_ffn_w_down"]})
    xs = run_block(build_L3_attn(), xs, {
        "wqT": np.ascontiguousarray(inp["L3_attn_W_Q"][:, :, 0].T),
        "wkT": np.ascontiguousarray(inp["L3_attn_W_K"][:, :, 0].T),
        "wvP": np.ascontiguousarray(inp["L3_attn_W_V"].transpose(1, 0, 2).reshape(D, D)),
        "woP": np.ascontiguousarray(inp["L3_attn_W_O"].transpose(2, 1, 0).reshape(D, D)),
        **l3a_aux()})
    xs = run_block(build_L3_ffn(), xs, {
        "gatew": inp["L3_ffn_gate_w"], "gateb": inp["L3_ffn_bias"],
        "upw": inp["L3_ffn_up_w"], "upb": inp["L3_ffn_up_b"],
        "downw": inp["L3_ffn_down_w"], "downb": inp["L3_ffn_down_b"],
        "swup": inp["L3_ffn_sw_up"], "sbup": inp["L3_ffn_sb_up"],
        "swdown": inp["L3_ffn_sw_down"], "sbdown": inp["L3_ffn_sb_down"]})
    return unshard_x(xs)
